# version 01
# baseline (speedup 1.0000x reference)
import numpy as np
import concourse.bass as bass
import concourse.mybir as mybir

F32 = mybir.dt.float32
BF16 = mybir.dt.bfloat16
I32 = mybir.dt.int32
AF = mybir.ActivationFunctionType
OP = mybir.AluOpType

D_MODEL = 2048
DSL = 256
EPS = 1e-5


import os
SERIALIZE = bool(int(os.environ.get("MK_SERIALIZE", "0")))
CHAIN = set(x for x in os.environ.get("MK_CHAIN", "vector,scalar,gpsimd").split(",") if x)


class Prog:
    ENGINES = ("sync", "vector", "scalar", "tensor", "gpsimd")

    def __init__(self, nc):
        self.nc = nc
        self.q = {e: [] for e in self.ENGINES}
        self.cnt = {}
        self.esem = {}
        self.last_evt = None

    def sem(self, name):
        s = self.nc.alloc_semaphore(name)
        self.cnt[s] = 0
        return s

    force_chain = ()

    def op(self, eng, fn, sem=None, amt=1, waits=()):
        waits = tuple(w for w in waits if w is not None)
        if (eng in CHAIN or eng in self.force_chain) and not SERIALIZE:
            if eng not in self.esem:
                self.esem[eng] = self.sem(f"ser_{eng}")
                self.prev_eng_evt = getattr(self, "prev_eng_evt", {})
            if sem is None:
                sem = self.esem[eng]
                amt = 16 if eng in ("sync",) else 1
            pe = self.prev_eng_evt.get(eng)
            if pe is not None:
                waits = waits + (pe,)
            self.cnt[sem] += amt
            evt = (sem, self.cnt[sem])
            self.prev_eng_evt[eng] = evt

            def run(e, fn=fn, waits=waits, sem=sem, amt=amt):
                for s, vv in waits:
                    e.wait_ge(s, vv)
                bi = fn(e)
                bi.then_inc(sem, amt)
            self.q[eng].append(run)
            return evt
        if SERIALIZE:
            if eng not in self.esem:
                self.esem[eng] = self.sem(f"ser_{eng}")
            if sem is None:
                sem = self.esem[eng]
                amt = 16 if eng in ("sync", "gpsimd") else 1
                if eng == "gpsimd":
                    amt = 1 if getattr(fn, "_pool_compute", False) else 16
            if self.last_evt is not None:
                waits = waits + (self.last_evt,)
        evt = None
        if sem is not None:
            self.cnt[sem] += amt
            evt = (sem, self.cnt[sem])
        if SERIALIZE:
            self.last_evt = evt

        def run(e, fn=fn, waits=waits, sem=sem, amt=amt):
            for s, vv in waits:
                e.wait_ge(s, vv)
            bi = fn(e)
            if sem is not None:
                bi.then_inc(sem, amt)
        self.q[eng].append(run)
        return evt

    def dma(self, eng, slot, fn, waits=()):
        if not hasattr(self, "dsem"):
            self.dsem = {}
        if slot not in self.dsem:
            self.dsem[slot] = self.sem(f"d_{slot}")
        return self.op(eng, fn, sem=self.dsem[slot], amt=16, waits=waits)

    def barrier_wait(self, eng, waits):
        waits = tuple(w for w in waits if w is not None)

        def run(e, waits=waits):
            for s, vv in waits:
                e.wait_ge(s, vv)
        self.q[eng].append(run)

    def emit(self, block):
        for name in self.ENGINES:
            if not self.q[name]:
                continue
            fns = self.q[name]

            def body(eng, fns=fns):
                for f in fns:
                    f(eng)
            getattr(block, name)(body)


def stt_int(eng, out, in0, scalar, in1, op0, op1):
    bi = eng.scalar_tensor_tensor(out, in0, float(scalar), in1, op0, op1)
    lst = list(bi.ins.ins)
    lst[1] = mybir.ImmediateValue(value=int(scalar), dtype=I32)
    bi.ins.replace_ins(lst)
    return bi


def host_inputs_l1(x, wp_full, wdt_full, bias_full, alog_full, dvec_full, core, T=None):
    import ml_dtypes
    T = x.shape[0] if T is None else T
    xt = np.ascontiguousarray(x[:T].T)
    lo, hi = DSL * core, DSL * (core + 1)
    d = {
        "xt": xt,
        "xts": np.ascontiguousarray(xt[lo:hi]),
        "wp": np.ascontiguousarray(wp_full),
        "wdt": np.ascontiguousarray(wdt_full[:, lo:hi]),
        "bias2": np.ascontiguousarray(bias_full[lo:hi].reshape(2, 128).T),
        "alog2": np.ascontiguousarray(
            alog_full[lo:hi].reshape(2, 128, 16).transpose(1, 0, 2).reshape(128, 32)),
        "dvec2": np.ascontiguousarray(dvec_full[lo:hi].reshape(2, 128).T),
        "oneh": np.concatenate([
            np.kron(np.eye(16, dtype=np.float32), np.ones((1, 128), np.float32)),
            np.zeros((16, 2048), np.float32),
            np.kron(np.eye(16, dtype=np.float32), np.ones((1, 128), np.float32)),
        ], axis=0).astype(ml_dtypes.bfloat16),
        "identb": np.eye(128, dtype=np.float32).astype(ml_dtypes.bfloat16),
        "onesk": np.ones((128, 1), np.float32).astype(ml_dtypes.bfloat16),
        "ones1f": np.ones((33, 128), np.float32),
    }
    return d


def build_l1(T):
    TC = 1024
    NCH = T // TC
    NF = T // 512
    HV = TC // 512
    nc = bass.Bass("TRN2", target_bir_lowering=False, debug=False)

    xt = nc.dram_tensor("xt", [D_MODEL, T], F32, kind="ExternalInput").ap()
    xts = nc.dram_tensor("xts", [DSL, T], F32, kind="ExternalInput").ap()
    wp = nc.dram_tensor("wp", [D_MODEL, 96], F32, kind="ExternalInput").ap()
    wdt = nc.dram_tensor("wdt", [64, DSL], F32, kind="ExternalInput").ap()
    bias2 = nc.dram_tensor("bias2", [128, 2], F32, kind="ExternalInput").ap()
    alog2 = nc.dram_tensor("alog2", [128, 32], F32, kind="ExternalInput").ap()
    dvec2 = nc.dram_tensor("dvec2", [128, 2], F32, kind="ExternalInput").ap()
    oneh = nc.dram_tensor("oneh", [48, 2048], BF16, kind="ExternalInput").ap()
    identb = nc.dram_tensor("identb", [128, 128], BF16, kind="ExternalInput").ap()
    onesk = nc.dram_tensor("onesk", [128, 1], BF16, kind="ExternalInput").ap()
    ones1f = nc.dram_tensor("ones1f", [33, 128], F32, kind="ExternalInput").ap()
    x2t = nc.dram_tensor("x2t", [DSL, T], F32, kind="ExternalOutput").ap()
    DEBUG = bool(int(os.environ.get("MK_DEBUG", "0")))
    if DEBUG:
        dbg_xdbl = nc.dram_tensor("dbg_xdbl", [96, T], F32, kind="ExternalOutput").ap()
        dbg_dl = nc.dram_tensor("dbg_dl", [DSL, T], F32, kind="ExternalOutput").ap()
        dbg_dh = nc.dram_tensor("dbg_dh", [DSL, T], F32, kind="ExternalOutput").ap()
        dbg_slab = nc.dram_tensor("dbg_slab", [48, T], F32, kind="ExternalOutput").ap()

    A = nc.alloc_sbuf_tensor
    xdbl = A("xdbl", [96, T], F32)
    slab = A("slab", [128, T], F32)
    slabbf = A("slabbf", [48, T], BF16)
    s_wpb = A("s_wpb", [128, 16 * 96], BF16)
    s_wdt = A("s_wdt", [64, DSL], F32)
    s_bias = A("s_bias", [128, 2], F32)
    s_alog = A("s_alog", [128, 32], F32)
    s_at = A("s_at", [128, 32], F32)
    s_dvec = A("s_dvec", [128, 2], F32)
    s_oneh = A("s_oneh", [48, 2048], BF16)
    s_idb = A("s_idb", [128, 128], BF16)
    s_onesk = A("s_onesk", [128, 1], BF16)
    s_ones1f = A("s_ones1f", [33, 128], F32)
    s_negS = A("s_negS", [1, 96], BF16)
    s_row = A("s_row", [1, 512], F32)
    s_rowbf = A("s_rowbf", [1, 512], BF16)
    s_xf = [A(f"s_xf{i}", [128, 512], F32) for i in range(2)]
    s_xb = [A(f"s_xb{i}", [128, 512], BF16) for i in range(2)]
    s_xq = [A(f"s_xq{i}", [128, 512], BF16) for i in range(2)]
    s_pe = A("s_pe", [96, 512], F32)
    s_xs = [A(f"s_xs{b}", [128, TC], F32) for b in range(2)]
    s_h1 = [A(f"s_h1{b}", [128, TC], F32) for b in range(2)]
    s_dl = [A(f"s_dl{b}", [128, TC], F32) for b in range(2)]
    s_dh = [A(f"s_dh{b}", [128, TC], F32) for b in range(2)]
    s_mub = A("s_mub", [128, TC], F32)
    s_rbb = A("s_rbb", [128, TC], F32)
    s_Bb = [A(f"s_Bb{i}", [128, TC], F32) for i in range(2)]
    s_Cb = [A(f"s_Cb{i}", [128, TC], F32) for i in range(2)]
    s_a = [A(f"s_a{i}", [128, TC], F32) for i in range(2)]
    s_bb = [A(f"s_bb{i}", [128, TC], F32) for i in range(2)]
    s_stt = [A(f"s_stt{i}", [128, TC], F32) for i in range(2)]
    s_sc = [A(f"s_sc{i}", [128, TC], BF16) for i in range(2)]
    s_car = A("s_car", [128, 32], F32)
    s_x2 = [A(f"s_x2{b}", [128, TC], F32) for b in range(2)]
    s_sp = [A(f"s_sp{i}", [128, 512], F32) for i in range(4)]

    PS = nc.alloc_psum_tensor
    bank = [PS(f"bank{i}", [128, 512], F32) for i in range(8)]
    p_P, p_sx, p_sq, p_aux = bank[0], bank[1], bank[2], bank[3]
    p_bc = [bank[0], bank[1]]
    p_d = bank[2]
    p_y = [[bank[4], bank[5]], [bank[6], bank[7]]]

    s_const = A("s_const", [128, 2], F32)

    pr = Prog(nc)
    sL = pr.sem("sL"); sG = pr.sem("sG"); sT = pr.sem("sT")
    sV = pr.sem("sV"); sS = pr.sem("sS"); sO = pr.sem("sO")

    pr.op("vector", lambda e: e.memset(s_const[:, 0:1], -1.0))
    e_cst = pr.op("vector", lambda e: e.memset(s_const[:, 1:2], -0.5), sem=sV)

    e_par = None
    for t, src in ((s_wdt, wdt), (s_bias, bias2), (s_alog, alog2), (s_dvec, dvec2),
                   (s_oneh, oneh), (s_idb, identb), (s_onesk, onesk), (s_ones1f, ones1f)):
        e_par = pr.dma("sync", "par", lambda e, t=t, src=src: e.dma_start(out=t[:], in_=src))
    e_wpb = None
    for k in range(16):
        e_wl = pr.dma("sync", "xf0", lambda e, k=k: e.dma_start(
            out=s_xf[0][:, 0:96], in_=wp[128 * k:128 * (k + 1), :]), waits=[e_wpb])
        e_wpb = pr.op("vector", lambda e, k=k: e.tensor_copy(
            s_wpb[:, 96 * k:96 * (k + 1)], s_xf[0][:, 0:96]), sem=sV, waits=[e_wl])
    pr.op("scalar", lambda e: e.activation(s_at[:], s_alog[:], AF.Exp), waits=[e_par])
    e_at = pr.op("scalar", lambda e: e.mul(s_at[:], s_at[:], -1.0), sem=sS)

    if os.environ.get("MK_CHAINT_S1"):
        pr.force_chain = ("tensor",)
    e_st = None
    for k in range(16):
        e_st = pr.op("tensor", lambda e, k=k: e.matmul(
            p_aux[0:1, 0:96], s_onesk[:, :], s_wpb[:, 96 * k:96 * (k + 1)],
            start=(k == 0), stop=(k == 15)), sem=sT, waits=([e_wpb] if k == 0 else ()))
    e_negS = pr.op("vector", lambda e: e.tensor_scalar(
        s_negS[:], p_aux[0:1, 0:96], -1.0, None, OP.mult), sem=sV, waits=[e_st])

    e_xb = {}
    e_xbq = {}
    e_mmb = {}
    e_mmq = {}
    e_fprev = {"xdbl": None, "r": None, "psub": None}
    for f in range(NF):
        col = slice(512 * f, 512 * (f + 1))
        e_mm_sq = None
        for k in range(16):
            bi_ = k % 2
            wld = [e_xb.get((f, k - 2)), e_xbq.get((f, k - 2))]
            if k < 2 and f > 0:
                wld = [e_xb.get((f - 1, 14 + k)), e_xbq.get((f - 1, 14 + k))]
            if f == 0 and k == 0:
                wld = [e_wpb]
            e_ld = pr.dma("sync", f"xf{bi_}", lambda e, k=k, col=col, bi_=bi_: e.dma_start(
                out=s_xf[bi_][:], in_=xt[128 * k:128 * (k + 1), col]), waits=wld)
            wcast = [e_mmb.get((f, k - 2)) if k >= 2 else (e_mmb.get((f - 1, 14 + k)) if f else None)]
            wsq = [e_mmq.get((f, k - 2)) if k >= 2 else (e_mmq.get((f - 1, 14 + k)) if f else None)]
            e_xb[(f, k)] = pr.op("vector", lambda e, bi_=bi_: e.tensor_copy(
                s_xb[bi_][:], s_xf[bi_][:]), sem=sV, waits=[e_ld] + wcast)
            e_xbq[(f, k)] = pr.op("scalar", lambda e, bi_=bi_: e.activation(
                s_xq[bi_][:], s_xf[bi_][:], AF.Square), sem=sS, waits=[e_ld] + wsq)
            wmm = [e_xb[(f, k)]]
            if k == 0:
                wmm += [e_negS, e_fprev["xdbl"]]
            pr.op("tensor", lambda e, k=k, bi_=bi_: e.matmul(
                p_P[0:96, :], s_wpb[:, 96 * k:96 * (k + 1)], s_xb[bi_][:],
                start=(k == 0), stop=False), sem=sT, waits=wmm)
            e_mmb[(f, k)] = pr.op("tensor", lambda e, k=k, bi_=bi_: e.matmul(
                p_sx[0:1, :], s_onesk[:, :], s_xb[bi_][:],
                start=(k == 0), stop=(k == 15)), sem=sT)
            e_mm_sq = e_mmq[(f, k)] = pr.op("tensor", lambda e, k=k, bi_=bi_: e.matmul(
                p_sq[0:1, :], s_onesk[:, :], s_xq[bi_][:],
                start=(k == 0), stop=(k == 15)), sem=sT, waits=[e_xbq[(f, k)]])
        e_mu = pr.op("vector", lambda e, col=col: e.tensor_scalar(
            slab[0:1, col], p_sx[0:1, :], 1.0 / D_MODEL, None, OP.mult), sem=sV, waits=[e_mm_sq])
        pr.op("vector", lambda e, col=col: e.tensor_tensor(
            s_row[:], slab[0:1, col], slab[0:1, col], OP.mult), waits=[e_fprev["r"]])
        e_var = pr.op("vector", lambda e: e.scalar_tensor_tensor(
            s_row[:], p_sq[0:1, :], 1.0 / D_MODEL, s_row[:], OP.mult, OP.subtract), sem=sV)
        e_mubf = pr.op("vector", lambda e, col=col: e.tensor_copy(
            s_rowbf[:], slab[0:1, col]), sem=sV, waits=[e_fprev["psub"]])
        e_var = pr.op("vector", lambda e: e.tensor_scalar(
            s_row[:], s_row[:], EPS, None, OP.add), sem=sV)
        pr.op("scalar", lambda e: e.activation(s_row[:], s_row[:], AF.Ln), waits=[e_var, e_cst])
        e_r = pr.op("scalar", lambda e, col=col: e.activation(
            slab[32:33, col], s_row[:], AF.Exp, scale=s_const[0:1, 1:2]), sem=sS)
        e_psub = pr.op("tensor", lambda e: e.matmul(
            p_P[0:96, :], s_negS[:, :], s_rowbf[:, :], start=False, stop=True), sem=sT,
            waits=[e_mubf])
        e_fprev["psub"] = e_psub
        e_fprev["r"] = e_r
        e_rb = pr.op("tensor", lambda e, col=col: e.matmul(
            p_aux[0:96, :], s_ones1f[32:33, 0:96], slab[32:33, col], start=True, stop=True), sem=sT,
            waits=[e_r, (e_negS if f == 0 else e_fprev["xdbl"])])
        e_pev = pr.op("vector", lambda e: e.tensor_copy(s_pe[:], p_P[0:96, :]), sem=sV,
                      waits=[e_psub])
        e_fprev["xdbl"] = pr.op("vector", lambda e, col=col: e.tensor_tensor(
            xdbl[:, col], s_pe[:], p_aux[0:96, :], OP.mult), sem=sV, waits=[e_rb])

    e_s1 = e_fprev["xdbl"]
    e_Bbf = pr.dma("gpsimd", "bbf", lambda e: e.dma_start(out=slabbf[0:16, :], in_=xdbl[64:80, :]),
                   waits=[e_s1])
    e_Cbf = pr.dma("gpsimd", "cbf", lambda e: e.dma_start(out=slabbf[32:48, :], in_=xdbl[80:96, :]),
                   waits=[e_s1])
    if DEBUG:
        pr.dma("gpsimd", "dbg", lambda e: e.dma_start(out=dbg_xdbl, in_=xdbl[:, :]), waits=[e_s1])
        pr.dma("gpsimd", "dbg", lambda e: e.dma_start(out=dbg_slab[0:1, :], in_=slab[0:1, :]), waits=[e_s1])
        pr.dma("gpsimd", "dbg", lambda e: e.dma_start(out=dbg_slab[32:33, :], in_=slab[32:33, :]), waits=[e_s1])

    pr.force_chain = ()
    last = {}
    e_out = []
    for c in range(NCH):
        e_xsld = {}
        for b in range(2):
            e_xsld[b] = pr.dma("sync", f"xs{b}", lambda e, b=b, c=c: e.dma_start(
                out=s_xs[b][:], in_=xts[128 * b:128 * (b + 1), TC * c:TC * (c + 1)]),
                waits=[last.get(("x2", b)), e_s1 if c == 0 else None])
        e_mue = e_rbe = None
        for h in range(HV):
            scol = slice(TC * c + 512 * h, TC * c + 512 * (h + 1))
            w0 = [e_s1 if c == 0 else None, last.get(("bc", 0)), last.get(("bc", 1))]
            em = pr.op("tensor", lambda e, scol=scol: e.matmul(
                p_bc[0][:, :], s_ones1f[0:1, :], slab[0:1, scol], start=True, stop=True),
                sem=sT, waits=w0)
            er = pr.op("tensor", lambda e, scol=scol: e.matmul(
                p_bc[1][:, :], s_ones1f[32:33, :], slab[32:33, scol], start=True, stop=True),
                sem=sT)
            e_mue = pr.op("vector", lambda e, h=h: e.tensor_copy(
                s_mub[:, 512 * h:512 * (h + 1)], p_bc[0][:, :]), sem=sV, waits=[em])
            e_rbe = pr.op("vector", lambda e, h=h: e.tensor_copy(
                s_rbb[:, 512 * h:512 * (h + 1)], p_bc[1][:, :]), sem=sV, waits=[er])
            last[("bc", 0)] = e_mue
            last[("bc", 1)] = e_rbe
        e_h1 = {}
        for b in range(2):
            pr.op("vector", lambda e, b=b: e.tensor_tensor(
                s_h1[b][:], s_xs[b][:], s_mub[:], OP.subtract),
                waits=[e_xsld[b], e_mue, e_rbe, last.get(("h1", b))])
            e_h1[b] = pr.op("vector", lambda e, b=b: e.tensor_tensor(
                s_h1[b][:], s_h1[b][:], s_rbb[:], OP.mult), sem=sV)
        e_dh = {}
        e_dl_done = {}
        for b in range(2):
            e_dlh = []
            for h in range(HV):
                scol = slice(TC * c + 512 * h, TC * c + 512 * (h + 1))
                e_dmm = pr.op("tensor", lambda e, b=b, scol=scol: e.matmul(
                    p_d[:, :], s_wdt[:, 128 * b:128 * (b + 1)], xdbl[0:64, scol],
                    start=True, stop=True), sem=sT,
                    waits=[e_s1 if c == 0 else None, last.get("p_d"), e_par])
                e_v1 = pr.op("vector", lambda e, b=b: e.tensor_scalar(
                    s_sp[0][:], p_d[:, :], s_bias[:, b:b + 1], None, OP.add),
                    sem=sV, waits=[e_dmm, last.get("spl")])
                last["p_d"] = e_v1
                e_nv = pr.op("vector", lambda e: e.scalar_tensor_tensor(
                    s_sp[1][:], s_sp[0][:], -1.0, s_sp[0][:], OP.mult, OP.min), sem=sV)
                e_mx = pr.op("vector", lambda e: e.tensor_scalar(
                    s_sp[3][:], s_sp[0][:], 0.0, None, OP.max), sem=sV)
                pr.op("scalar", lambda e: e.activation(s_sp[2][:], s_sp[1][:], AF.Exp),
                      waits=[e_nv])
                e_l = pr.op("scalar", lambda e: e.activation(
                    s_sp[1][:], s_sp[2][:], AF.Ln, bias=1.0), sem=sS)
                last["spl"] = e_l
                e_dlh.append(pr.op("vector", lambda e, b=b, h=h: e.tensor_tensor(
                    s_dl[b][:, 512 * h:512 * (h + 1)], s_sp[1][:], s_sp[3][:], OP.add),
                    sem=sV, waits=[e_l, e_mx, last.get(("dlr", b))]))
            e_dl_done[b] = e_dlh[-1]
            e_dh[b] = pr.op("vector", lambda e, b=b: e.tensor_tensor(
                s_dh[b][:], s_dl[b][:], s_h1[b][:], OP.mult), sem=sV,
                waits=[e_h1[b]] + e_dlh + [last.get(("dhr", b))])
            if DEBUG:
                pr.dma("gpsimd", f"dbgdl{b}", lambda e, b=b, c=c: e.dma_start(
                    out=dbg_dl[128 * b:128 * (b + 1), TC * c:TC * (c + 1)], in_=s_dl[b][:]),
                    waits=[e_dh[b]])
                pr.dma("gpsimd", f"dbgdh{b}", lambda e, b=b, c=c: e.dma_start(
                    out=dbg_dh[128 * b:128 * (b + 1), TC * c:TC * (c + 1)], in_=s_dh[b][:]),
                    waits=[e_dh[b]])
        for n in range(16):
            pbn = n % 2
            e_Bev = e_Cev = None
            for h in range(HV):
                wB = [e_Bbf, last.get(("bc", pbn))] if h == 0 else [e_Bev]
                if h == 0:
                    wB.append(last.get(("Bb", pbn)))
                eb = pr.op("tensor", lambda e, n=n, c=c, h=h, pbn=pbn: e.matmul(
                    p_bc[pbn][:, :], s_oneh[0:16, 128 * n:128 * (n + 1)],
                    slabbf[0:16, TC * c + 512 * h:TC * c + 512 * (h + 1)],
                    start=True, stop=True), sem=sT, waits=wB)
                e_Bev = pr.op("vector", lambda e, h=h, pbn=pbn: e.tensor_copy(
                    s_Bb[pbn][:, 512 * h:512 * (h + 1)], p_bc[pbn][:, :]), sem=sV,
                    waits=[eb])
            for h in range(HV):
                wC = [e_Bev, e_Cbf, last.get(("Cb", pbn))] if h == 0 else [e_Cev]
                ecm = pr.op("tensor", lambda e, n=n, c=c, h=h, pbn=pbn: e.matmul(
                    p_bc[pbn][:, :], s_oneh[32:48, 128 * n:128 * (n + 1)],
                    slabbf[32:48, TC * c + 512 * h:TC * c + 512 * (h + 1)],
                    start=True, stop=True), sem=sT, waits=wC)
                e_Cev = pr.op("vector", lambda e, h=h, pbn=pbn: e.tensor_copy(
                    s_Cb[pbn][:, 512 * h:512 * (h + 1)], p_bc[pbn][:, :]), sem=sV, waits=[ecm])
            last[("bc", pbn)] = e_Cev
            for b in range(2):
                sidx = 2 * n + b
                rb = sidx % 2
                e_a = pr.op("scalar", lambda e, b=b, n=n, rb=rb: e.activation(
                    s_a[rb][:], s_dl[b][:], AF.Exp, scale=s_at[:, 16 * b + n:16 * b + n + 1]),
                    sem=sS, waits=[e_dl_done[b], e_at, last.get(("a", rb))])
                last[("dlr", b)] = e_a
                e_bm = pr.op("gpsimd", lambda e, b=b, rb=rb, pbn=pbn: e.tensor_tensor(
                    s_bb[rb][:], s_dh[b][:], s_Bb[pbn][:], OP.mult), sem=sO,
                    waits=[e_dh[b], e_Bev, last.get(("b", rb))])
                last[("Bb", pbn)] = e_bm
                last[("dhr", b)] = e_bm
                init = 0.0 if c == 0 else s_car[:, sidx:sidx + 1]
                e_scn = pr.op("vector", lambda e, rb=rb, init=init: e.tensor_tensor_scan(
                    s_stt[rb][:], s_a[rb][:], s_bb[rb][:], init, OP.mult, OP.add), sem=sV,
                    waits=[e_a, e_bm, last.get(("st", rb))])
                last[("a", rb)] = e_scn
                last[("b", rb)] = e_scn
                pr.op("vector", lambda e, rb=rb, sidx=sidx: e.tensor_copy(
                    s_car[:, sidx:sidx + 1], s_stt[rb][:, TC - 1:TC]))
                e_scc = pr.op("vector", lambda e, rb=rb, pbn=pbn: e.tensor_tensor(
                    s_sc[rb][:], s_stt[rb][:], s_Cb[pbn][:], OP.mult), sem=sV,
                    waits=[e_Cev, last.get(("sc", rb))])
                last[("st", rb)] = e_scc
                last[("Cb", pbn)] = e_scc
                e_ymm = None
                for h in range(HV):
                    e_ymm = pr.op("tensor", lambda e, b=b, h=h, rb=rb, n=n: e.matmul(
                        p_y[b][h][:, :], s_idb[:, :], s_sc[rb][:, 512 * h:512 * (h + 1)],
                        start=(n == 0), stop=(n == 15)), sem=sT,
                        waits=[e_scc, e_par] + ([last.get(("y", b, h))] if n == 0 else []))
                last[("sc", rb)] = e_ymm
                if n == 15:
                    last[("ylast", b)] = e_ymm
        for b in range(2):
            e_v1 = {}
            for h in range(HV):
                e_v1[h] = pr.op("vector", lambda e, b=b, h=h: e.scalar_tensor_tensor(
                    s_x2[b][:, 512 * h:512 * (h + 1)], s_h1[b][:, 512 * h:512 * (h + 1)],
                    s_dvec[:, b:b + 1], p_y[b][h][:, :], OP.mult, OP.add), sem=sV,
                    waits=[last[("ylast", b)], last.get(("x2st", b))])
                last[("y", b, h)] = e_v1[h]
            e_x2 = pr.op("vector", lambda e, b=b: e.tensor_tensor(
                s_x2[b][:], s_x2[b][:], s_xs[b][:], OP.add), sem=sV,
                waits=[e_v1[h] for h in range(HV)])
            last[("x2", b)] = e_x2
            e_st2 = pr.dma("gpsimd", f"x2st{b}", lambda e, b=b, c=c: e.dma_start(
                out=x2t[128 * b:128 * (b + 1), TC * c:TC * (c + 1)], in_=s_x2[b][:]),
                waits=[e_x2])
            last[("x2st", b)] = e_st2
            e_out.append(e_st2)
    pr.barrier_wait("gpsimd", e_out[-2:])
    pr.barrier_wait("vector", e_out[-2:])

    with nc.Block() as block:
        pr.emit(block)
    return nc


def host_inputs_l2(x2t_full, gwp, gws, uwp, uws, dwp, dws, core, T=None):
    import ml_dtypes
    T = x2t_full.shape[1] if T is None else T
    nlo, nhi = 2048 * core, 2048 * (core + 1)
    return {
        "x2t": x2t_full,
        "wg_p": np.ascontiguousarray(gwp[nlo:nhi]),
        "wg_s": np.ascontiguousarray(gws[nlo:nhi]),
        "wu_p": np.ascontiguousarray(uwp[nlo:nhi]),
        "wu_s": np.ascontiguousarray(uws[nlo:nhi]),
        "wd_p": np.ascontiguousarray(dwp[:, 1024 * core:1024 * (core + 1)]),
        "wd_s": np.ascontiguousarray(dws[:, 64 * core:64 * (core + 1)]),
        "identb": np.eye(128, dtype=np.float32).astype(ml_dtypes.bfloat16),
        "onesk": np.ones((128, 1), np.float32).astype(ml_dtypes.bfloat16),
        "ones1f": np.ones((33, 128), np.float32),
    }


def build_l2(T):
    TCF = 1024
    NCH = T // TCF
    NF = T // 512
    nc = bass.Bass("TRN2", target_bir_lowering=False, debug=False)

    x2t = nc.dram_tensor("x2t", [D_MODEL, T], F32, kind="ExternalInput").ap()
    wg_p = nc.dram_tensor("wg_p", [2048, 1024], I32, kind="ExternalInput").ap()
    wg_s = nc.dram_tensor("wg_s", [2048, 64], F32, kind="ExternalInput").ap()
    wu_p = nc.dram_tensor("wu_p", [2048, 1024], I32, kind="ExternalInput").ap()
    wu_s = nc.dram_tensor("wu_s", [2048, 64], F32, kind="ExternalInput").ap()
    wd_p = nc.dram_tensor("wd_p", [2048, 1024], I32, kind="ExternalInput").ap()
    wd_s = nc.dram_tensor("wd_s", [2048, 64], F32, kind="ExternalInput").ap()
    identb = nc.dram_tensor("identb", [128, 128], BF16, kind="ExternalInput").ap()
    onesk = nc.dram_tensor("onesk", [128, 1], BF16, kind="ExternalInput").ap()
    ones1f = nc.dram_tensor("ones1f", [33, 128], F32, kind="ExternalInput").ap()
    pout = nc.dram_tensor("pout", [D_MODEL, T], F32, kind="ExternalOutput").ap()
    wgt_d = nc.dram_tensor("wgt_d", [16, 128, 16, 128], BF16).ap()
    wut_d = nc.dram_tensor("wut_d", [16, 128, 16, 128], BF16).ap()
    wdt_d = nc.dram_tensor("wdt_d", [16, 128, 16, 128], BF16).ap()

    A = nc.alloc_sbuf_tensor
    slab = A("slab", [128, T], F32)
    s_idb = A("s_idb", [128, 128], BF16)
    s_onesk = A("s_onesk", [128, 1], BF16)
    s_ones1f = A("s_ones1f", [33, 128], F32)
    s_row = A("s_row", [1, 512], F32)
    s_xf = [A(f"s_xf{i}", [128, 512], F32) for i in range(2)]
    s_xb = [A(f"s_xb{i}", [128, 512], BF16) for i in range(2)]
    s_xq = [A(f"s_xq{i}", [128, 512], BF16) for i in range(2)]
    s_pk = [A(f"s_pk{i}", [128, 1024], I32) for i in range(2)]
    s_dq = [A(f"s_dq{i}", [128, 1024], I32) for i in range(4)]
    s_scf = [A(f"s_scf{i}", [128, 64], F32) for i in range(2)]
    s_scb = A("s_scb", [128, 64], BF16)
    s_srep = A("s_srep", [128, 2048], BF16)
    s_wrow = [A(f"s_wrow{i}", [128, 2048], BF16) for i in range(2)]
    s_wev = [A(f"s_wev{i}", [128, 512], BF16) for i in range(2)]
    s_h2 = [A(f"s_h2{k}", [128, TCF], BF16) for k in range(16)]
    s_int = [A(f"s_int{k}", [128, TCF], BF16) for k in range(16)]
    s_mub = A("s_mub", [128, TCF], F32)
    s_rbb = A("s_rbb", [128, TCF], F32)
    s_xc = [A(f"s_xc{i}", [128, TCF], F32) for i in range(2)]
    s_wg = [A(f"s_wg{i}", [128, 2048], BF16) for i in range(2)]
    s_wu = [A(f"s_wu{i}", [128, 2048], BF16) for i in range(2)]
    s_wd = [A(f"s_wd{i}", [128, 2048], BF16) for i in range(2)]
    s_gel = [A(f"s_gel{i}", [128, 512], F32) for i in range(2)]
    s_ost = [A(f"s_ost{i}", [128, TCF], F32) for i in range(2)]

    PS = nc.alloc_psum_tensor
    b_f = [PS(f"bf{i}", [128, 512], F32) for i in range(6)]
    b_tr = [PS(f"btr{i}", [128, 1024], BF16) for i in range(2)]
    p_sx, p_sq = b_f[0], b_f[1]
    p_bc = [b_f[0], b_f[1]]
    p_g = [b_f[2], b_f[3]]
    p_u = [b_f[4], b_f[5]]
    p_o = [b_f[2], b_f[3]]

    s_cn1 = A("s_cn1", [128, 2], F32)

    pr = Prog(nc)
    sT = pr.sem("sT"); sV = pr.sem("sV"); sS = pr.sem("sS")
    pr.op("vector", lambda e: e.memset(s_cn1[:, 0:1], -1.0))
    pr.op("vector", lambda e: e.memset(s_cn1[:, 1:2], -0.5), sem=sV)

    e_par = None
    for t, src in ((s_idb, identb), (s_onesk, onesk), (s_ones1f, ones1f)):
        e_par = pr.dma("sync", "par", lambda e, t=t, src=src: e.dma_start(out=t[:], in_=src))

    e_xb = {}
    e_xq = {}
    e_mmb = {}
    e_mmq = {}
    e_r = {}
    e_gate_s1 = None
    for f in range(NF):
        col = slice(512 * f, 512 * (f + 1))
        e_mm_last = None
        for k in range(16):
            bi_ = k % 2
            wld = [e_xb.get((f, k - 2)), e_xq.get((f, k - 2))]
            if k < 2 and f > 0:
                wld = [e_xb.get((f - 1, 14 + k)), e_xq.get((f - 1, 14 + k))]
            e_ld = pr.dma("sync", f"xf{bi_}", lambda e, k=k, col=col, bi_=bi_: e.dma_start(
                out=s_xf[bi_][:], in_=x2t[128 * k:128 * (k + 1), col]), waits=wld)
            wc = [e_mmb.get((f, k - 2) if k >= 2 else ((f - 1, 14 + k) if f else None))]
            wq = [e_mmq.get((f, k - 2) if k >= 2 else ((f - 1, 14 + k) if f else None))]
            e_xb[(f, k)] = pr.op("vector", lambda e, bi_=bi_: e.tensor_copy(
                s_xb[bi_][:], s_xf[bi_][:]), sem=sV, waits=[e_ld] + wc)
            e_xq[(f, k)] = pr.op("scalar", lambda e, bi_=bi_: e.activation(
                s_xq[bi_][:], s_xf[bi_][:], AF.Square), sem=sS, waits=[e_ld] + wq)
            wmm = [e_xb[(f, k)]]
            if k == 0:
                wmm.append(e_r.get(f - 1))
                if f == 0:
                    wmm.append(e_par)
            e_mmb[(f, k)] = pr.op("tensor", lambda e, bi_=bi_, k=k: e.matmul(
                p_sx[0:1, :], s_onesk[:, :], s_xb[bi_][:],
                start=(k == 0), stop=(k == 15)), sem=sT, waits=wmm)
            e_mm_last = e_mmq[(f, k)] = pr.op("tensor", lambda e, bi_=bi_, k=k: e.matmul(
                p_sq[0:1, :], s_onesk[:, :], s_xq[bi_][:],
                start=(k == 0), stop=(k == 15)), sem=sT, waits=[e_xq[(f, k)]])
        e_mu = pr.op("vector", lambda e, col=col: e.tensor_scalar(
            slab[0:1, col], p_sx[0:1, :], 1.0 / D_MODEL, None, OP.mult), sem=sV, waits=[e_mm_last])
        pr.op("vector", lambda e, col=col: e.tensor_tensor(
            s_row[:], slab[0:1, col], slab[0:1, col], OP.mult))
        e_var = pr.op("vector", lambda e: e.scalar_tensor_tensor(
            s_row[:], p_sq[0:1, :], 1.0 / D_MODEL, s_row[:], OP.mult, OP.subtract), sem=sV)
        pr.op("vector", lambda e: e.tensor_scalar(s_row[:], s_row[:], EPS, None, OP.add), sem=sV)
        pr.op("scalar", lambda e: e.activation(s_row[:], s_row[:], AF.Ln), waits=[pr.prev_eng_evt["vector"]])
        e_r[f] = pr.op("scalar", lambda e, col=col: e.activation(
            slab[32:33, col], s_row[:], AF.Exp, scale=s_cn1[0:1, 1:2]), sem=sS)
    e_stats = e_r[NF - 1]

    def dequant_matrix(tag, pk_d, sc_d, dst_d, e_prev_dma):
        evs = {}
        for nr in range(16):
            pki = nr % 2
            wpk = [evs.get(("dec", nr - 2))]
            e_pk = pr.dma("gpsimd", f"pk{tag}{pki}", lambda e, nr=nr, pki=pki: e.dma_start(
                out=s_pk[pki][:], in_=pk_d[128 * nr:128 * (nr + 1), :]), waits=wpk)
            e_sc = pr.dma("gpsimd", f"sc{tag}", lambda e, nr=nr: e.dma_start(
                out=s_scf[nr % 2][:], in_=sc_d[128 * nr:128 * (nr + 1), :]),
                waits=[evs.get(("srep", nr - 1))])
            c = s_pk[pki]
            dA, dB, dC, dD = s_dq
            pr.op("vector", lambda e, c=c: e.tensor_scalar(dA[:], c[:], 7, None, OP.bitwise_and),
                  sem=sV, waits=[e_pk, evs.get(("tr", nr - 2))])
            pr.op("vector", lambda e: e.tensor_scalar(dB[:], dA[:], 252, None, OP.add))
            pr.op("vector", lambda e: e.scalar_tensor_tensor(dC[:], dA[:], 252.0, dB[:], OP.mult, OP.min))
            pr.op("vector", lambda e, c=c: e.tensor_scalar(dB[:], c[:], 8, 12, OP.bitwise_and, OP.logical_shift_left))
            pr.op("vector", lambda e: stt_int(e, dD[:], dC[:], 6, dB[:], OP.logical_shift_left, OP.bitwise_or))
            pr.op("vector", lambda e, c=c: e.tensor_scalar(dA[:], c[:], 4, 7, OP.logical_shift_right, OP.bitwise_and))
            pr.op("vector", lambda e: e.tensor_scalar(dC[:], dA[:], 252, None, OP.add))
            pr.op("vector", lambda e: e.scalar_tensor_tensor(dB[:], dA[:], 252.0, dC[:], OP.mult, OP.min))
            pr.op("vector", lambda e, c=c: e.tensor_scalar(dC[:], c[:], 128, 24, OP.bitwise_and, OP.logical_shift_left))
            pr.op("vector", lambda e: stt_int(e, dA[:], dB[:], 22, dC[:], OP.logical_shift_left, OP.bitwise_or))
            e_i32 = pr.op("vector", lambda e: e.tensor_tensor(dB[:], dD[:], dA[:], OP.bitwise_or), sem=sV)
            pr.op("vector", lambda e, nr=nr: e.tensor_copy(s_scb[:], s_scf[nr % 2][:]), waits=[e_sc])
            rep = s_scb[:].unsqueeze(2).broadcast_to([128, 64, 32])
            pr.op("vector", lambda e, rep=rep: e.tensor_copy(
                s_srep[:].rearrange("p (s r) -> p s r", r=32), rep))
            evs[("srep", nr)] = pr.prev_eng_evt["vector"]
            wri = nr % 2
            e_w = pr.op("vector", lambda e, wri=wri: e.tensor_tensor(
                s_wrow[wri][:], dB[:].bitcast(BF16), s_srep[:], OP.mult), sem=sV,
                waits=[evs.get(("trdone", nr - 2))])
            evs[("dec", nr)] = e_w
            e_st_ = None
            for g in range(4):
                tb = g % 2
                e_trs = None
                for j in range(4):
                    blk = 4 * g + j
                    wtr = [e_w, e_par]
                    if g >= 2:
                        wtr.append(evs.get(("ev", nr, g - 2)))
                    elif nr > 0:
                        wtr.append(evs.get(("ev", nr - 1, g + 2)))
                    e_trs = pr.op("tensor", lambda e, blk=blk, tb=tb, j=j, wri=wri: e.transpose(
                        b_tr[tb][:, 128 * j:128 * (j + 1)],
                        s_wrow[wri][:, 128 * blk:128 * (blk + 1)], s_idb[:, :]),
                        sem=sT, waits=(wtr if j == 0 else ()))
                wev = [e_trs]
                if g >= 2:
                    wev.append(evs.get(("st", nr, g - 2)))
                elif nr > 0:
                    wev.append(evs.get(("st", nr - 1, g + 2)))
                e_ev = pr.op("vector", lambda e, tb=tb: e.tensor_copy(
                    s_wev[tb][:], b_tr[tb][:, 0:512]), sem=sV, waits=wev)
                evs[("ev", nr, g)] = e_ev
                e_st_ = pr.dma("gpsimd", f"wst{tag}{tb}", lambda e, nr=nr, g=g, tb=tb: e.dma_start(
                    out=dst_d[nr, :, 4 * g:4 * g + 4, :].rearrange("p j c -> p (j c)"),
                    in_=s_wev[tb][:]), waits=[e_ev])
                evs[("st", nr, g)] = e_st_
            evs[("tr", nr)] = e_trs
            evs[("trdone", nr)] = e_st_
            evs[("tile", nr)] = e_st_
        return evs

    e_gdq = dequant_matrix("g", wg_p, wg_s, wgt_d, None)
    e_udq = dequant_matrix("u", wu_p, wu_s, wut_d, None)
    e_ddq = dequant_matrix("d", wd_p, wd_s, wdt_d, None)

    last = {}
    e_outst = []
    for ci in range(NCH):
        ccol = slice(TCF * ci, TCF * (ci + 1))
        e_mue = e_rbe = None
        for h in range(2):
            scol = slice(TCF * ci + 512 * h, TCF * ci + 512 * (h + 1))
            w0 = [e_stats, last.get(("bc", 0)), last.get(("bc", 1))]
            em = pr.op("tensor", lambda e, scol=scol: e.matmul(
                p_bc[0][:, :], s_ones1f[0:1, :], slab[0:1, scol], start=True, stop=True),
                sem=sT, waits=w0)
            er = pr.op("tensor", lambda e, scol=scol: e.matmul(
                p_bc[1][:, :], s_ones1f[32:33, :], slab[32:33, scol], start=True, stop=True),
                sem=sT)
            e_mue = pr.op("vector", lambda e, h=h: e.tensor_copy(
                s_mub[:, 512 * h:512 * (h + 1)], p_bc[0][:, :]), sem=sV, waits=[em])
            e_rbe = pr.op("vector", lambda e, h=h: e.tensor_copy(
                s_rbb[:, 512 * h:512 * (h + 1)], p_bc[1][:, :]), sem=sV, waits=[er])
            last[("bc", 0)] = e_mue
            last[("bc", 1)] = e_rbe
        e_h2 = {}
        for k in range(16):
            xi = k % 2
            e_xl = pr.dma("sync", f"xc{xi}", lambda e, k=k, xi=xi, ccol=ccol: e.dma_start(
                out=s_xc[xi][:], in_=x2t[128 * k:128 * (k + 1), ccol]),
                waits=[e_h2.get(k - 2) if k >= 2 else last.get(("h2w", 14 + k))])
            pr.op("vector", lambda e, k=k, xi=xi: e.tensor_tensor(
                s_xc[xi][:], s_xc[xi][:], s_mub[:], OP.subtract),
                waits=[e_xl, e_mue, e_rbe, last.get(("h2r", k))])
            e_h2[k] = pr.op("vector", lambda e, k=k, xi=xi: e.tensor_tensor(
                s_h2[k][:], s_xc[xi][:], s_rbb[:], OP.mult), sem=sV)
            last[("h2w", k)] = e_h2[k]
        e_h2all = e_h2[15]
        e_int = {}
        for ns in range(16):
            wi = ns % 2
            e_wgl = pr.dma("sync", f"wg{wi}", lambda e, ns=ns, wi=wi: e.dma_start(
                out=s_wg[wi][:],
                in_=wgt_d[ns].rearrange("p j c -> p (j c)")),
                waits=[e_gdq[("tile", ns)] if ci == 0 else None, last.get(("wg", wi))])
            e_wul = pr.dma("sync", f"wu{wi}", lambda e, ns=ns, wi=wi: e.dma_start(
                out=s_wu[wi][:],
                in_=wut_d[ns].rearrange("p j c -> p (j c)")),
                waits=[e_udq[("tile", ns)] if ci == 0 else None, last.get(("wu", wi))])
            e_gmm = {}
            e_umm = {}
            for sub in range(2):
                for kb in range(16):
                    wmg = [e_wgl, e_h2all, last.get(("pg", sub))] if kb == 0 else ()
                    e_gmm[sub] = pr.op("tensor", lambda e, kb=kb, sub=sub, wi=wi: e.matmul(
                        p_g[sub][:, :], s_wg[wi][:, 128 * kb:128 * (kb + 1)],
                        s_h2[kb][:, 512 * sub:512 * (sub + 1)],
                        start=(kb == 0), stop=(kb == 15)), sem=sT, waits=wmg)
                for kb in range(16):
                    wmu = [e_wul, last.get(("pu", sub))] if kb == 0 else ()
                    e_umm[sub] = pr.op("tensor", lambda e, kb=kb, sub=sub, wi=wi: e.matmul(
                        p_u[sub][:, :], s_wu[wi][:, 128 * kb:128 * (kb + 1)],
                        s_h2[kb][:, 512 * sub:512 * (sub + 1)],
                        start=(kb == 0), stop=(kb == 15)), sem=sT, waits=wmu)
            last[("wg", wi)] = e_gmm[1]
            last[("wu", wi)] = e_umm[1]
            e_iv = None
            for sub in range(2):
                gl = sub % 2
                e_gact = pr.op("scalar", lambda e, gl=gl, sub=sub: e.activation(
                    s_gel[gl][:], p_g[sub][:, :], AF.Gelu_apprx_tanh), sem=sS,
                    waits=[e_gmm[sub], e_umm[sub], last.get(("gel", gl))])
                e_iv = pr.op("vector", lambda e, ns=ns, gl=gl, sub=sub: e.tensor_tensor(
                    s_int[ns][:, 512 * sub:512 * (sub + 1)], s_gel[gl][:], p_u[sub][:, :], OP.mult),
                    sem=sV, waits=[e_gact, last.get(("int", ns)) if sub == 0 else None])
                last[("gel", gl)] = e_iv
                last[("pg", sub)] = e_iv
                last[("pu", sub)] = e_iv
            e_int[ns] = e_iv
        e_intall = e_int[15]
        for hs in range(16):
            wi = hs % 2
            e_wdl = pr.dma("sync", f"wd{wi}", lambda e, hs=hs, wi=wi: e.dma_start(
                out=s_wd[wi][:],
                in_=wdt_d[hs].rearrange("p j c -> p (j c)")),
                waits=[e_ddq[("tile", hs)] if ci == 0 else None, last.get(("wd", wi))])
            e_omm = {}
            for sub in range(2):
                for nkb in range(16):
                    wmo = [e_wdl, e_intall, last.get(("po", sub))] if nkb == 0 else ()
                    e_omm[sub] = pr.op("tensor", lambda e, nkb=nkb, sub=sub, wi=wi: e.matmul(
                        p_o[sub][:, :], s_wd[wi][:, 128 * nkb:128 * (nkb + 1)],
                        s_int[nkb][:, 512 * sub:512 * (sub + 1)],
                        start=(nkb == 0), stop=(nkb == 15)), sem=sT, waits=wmo)
            last[("wd", wi)] = e_omm[1]
            oi = hs % 2
            e_oev = None
            for sub in range(2):
                e_oev = pr.op("vector", lambda e, oi=oi, sub=sub: e.tensor_copy(
                    s_ost[oi][:, 512 * sub:512 * (sub + 1)], p_o[sub][:, :]), sem=sV,
                    waits=[e_omm[sub], last.get(("ost", oi)) if sub == 0 else None])
                last[("po", sub)] = e_oev
            e_os = pr.dma("gpsimd", f"os{oi}", lambda e, hs=hs, oi=oi, ccol=ccol: e.dma_start(
                out=pout[128 * hs:128 * (hs + 1), ccol], in_=s_ost[oi][:]), waits=[e_oev])
            last[("ost", oi)] = e_os
            e_outst.append(e_os)
        e_down_last = last[("po", 1)]
        last[("pg", 0)] = last[("po", 0)]
        last[("pg", 1)] = last[("po", 1)]
        for k in range(16):
            last[("h2r", k)] = e_intall
            last[("int", k)] = e_down_last
    pr.barrier_wait("gpsimd", e_outst[-2:])
    pr.barrier_wait("vector", e_outst[-2:])

    with nc.Block() as block:
        pr.emit(block)
    return nc


def build_l3(T):
    PC = 2048
    NP = T // PC
    nc = bass.Bass("TRN2", target_bir_lowering=False, debug=False)
    parts = nc.dram_tensor("parts", [8, DSL, T], F32, kind="ExternalInput").ap()
    x2ts = nc.dram_tensor("x2ts", [DSL, T], F32, kind="ExternalInput").ap()
    outp = nc.dram_tensor("outp", [DSL, T], F32, kind="ExternalOutput").ap()

    A = nc.alloc_sbuf_tensor
    s_acc = [A(f"s_acc{i}", [128, PC], F32) for i in range(2)]
    s_ld = [A(f"s_ld{i}", [128, PC], F32) for i in range(3)]

    pr = Prog(nc)
    sV = pr.sem("sV")
    last = {}
    e_sts = []
    for b in range(2):
        for q in range(NP):
            col = slice(PC * q, PC * (q + 1))
            ai = (b * NP + q) % 2
            e_a = pr.dma("sync", f"acc{ai}", lambda e, b=b, col=col, ai=ai: e.dma_start(
                out=s_acc[ai][:], in_=parts[0, 128 * b:128 * (b + 1), col]),
                waits=[last.get(("accst", ai))])
            e_add = None
            for j in range(8):
                li = j % 3
                src = (parts[j + 1, 128 * b:128 * (b + 1), col] if j < 7
                       else x2ts[128 * b:128 * (b + 1), col])
                e_l = pr.dma("sync", f"ld{li}", lambda e, src=src, li=li: e.dma_start(
                    out=s_ld[li][:], in_=src), waits=[last.get(("ldr", li))])
                e_add = pr.op("vector", lambda e, ai=ai, li=li: e.tensor_tensor(
                    s_acc[ai][:], s_acc[ai][:], s_ld[li][:], OP.add), sem=sV,
                    waits=[e_l] + ([e_a] if j == 0 else []))
                last[("ldr", li)] = e_add
            e_st = pr.dma("gpsimd", f"st{ai}", lambda e, b=b, col=col, ai=ai: e.dma_start(
                out=outp[128 * b:128 * (b + 1), col], in_=s_acc[ai][:]), waits=[e_add])
            last[("accst", ai)] = e_st
            e_sts.append(e_st)
    pr.barrier_wait("gpsimd", e_sts[-2:])
    with nc.Block() as block:
        pr.emit(block)
    return nc


from concourse.bass_utils import run_bass_kernel_spmd

_BUILT = {}


def _get(builder, T):
    key = (builder.__name__, T)
    if key not in _BUILT:
        _BUILT[key] = builder(T)
    return _BUILT[key]


def kernel(L0_attn_x, L0_attn_x_proj_weight, L0_attn_dt_proj_weight, L0_attn_dt_proj_bias,
           L0_attn_A_log, L0_attn_D, L0_ffn_gate_W_packed, L0_ffn_gate_scales,
           L0_ffn_up_W_packed, L0_ffn_up_scales, L0_ffn_down_W_packed, L0_ffn_down_scales):
    x = np.asarray(L0_attn_x, dtype=np.float32)[0]
    T = x.shape[0]
    wp = np.asarray(L0_attn_x_proj_weight, np.float32)
    wdt = np.asarray(L0_attn_dt_proj_weight, np.float32)
    bias = np.asarray(L0_attn_dt_proj_bias, np.float32)
    alog = np.asarray(L0_attn_A_log, np.float32)
    dvec = np.asarray(L0_attn_D, np.float32)
    gwp = np.asarray(L0_ffn_gate_W_packed, np.int32)
    gws = np.asarray(L0_ffn_gate_scales, np.float32)
    uwp = np.asarray(L0_ffn_up_W_packed, np.int32)
    uws = np.asarray(L0_ffn_up_scales, np.float32)
    dwp = np.asarray(L0_ffn_down_W_packed, np.int32)
    dws = np.asarray(L0_ffn_down_scales, np.float32)

    nc1 = _get(build_l1, T)
    in1 = [host_inputs_l1(x, wp, wdt, bias, alog, dvec, c, T) for c in range(8)]
    r1 = run_bass_kernel_spmd(nc1, in1, core_ids=list(range(8)))
    x2t_full = np.ascontiguousarray(
        np.concatenate([r1.results[c]["x2t"] for c in range(8)], axis=0))

    nc2 = _get(build_l2, T)
    in2 = [host_inputs_l2(x2t_full, gwp, gws, uwp, uws, dwp, dws, c, T) for c in range(8)]
    r2 = run_bass_kernel_spmd(nc2, in2, core_ids=list(range(8)))
    pouts = [np.asarray(r2.results[c]["pout"]) for c in range(8)]

    nc3 = _get(build_l3, T)
    in3 = []
    for c in range(8):
        lo, hi = DSL * c, DSL * (c + 1)
        in3.append({
            "parts": np.ascontiguousarray(np.stack([pouts[r][lo:hi] for r in range(8)])),
            "x2ts": np.ascontiguousarray(x2t_full[lo:hi]),
        })
    r3 = run_bass_kernel_spmd(nc3, in3, core_ids=list(range(8)))
    out_t = np.concatenate([r3.results[c]["outp"] for c in range(8)], axis=0)
    return np.ascontiguousarray(out_t.T)[None, :, :].astype(np.float32)
